# version 01
# baseline (speedup 1.0000x reference)
import math
from contextlib import ExitStack

import numpy as np
import ml_dtypes
import concourse.bass as bass
import concourse.mybir as mybir
from concourse.bass_utils import run_bass_kernel_spmd

F32 = mybir.dt.float32
BF16 = mybir.dt.bfloat16
AF = mybir.ActivationFunctionType
ALU = mybir.AluOpType
AX = mybir.AxisListType

B, S, D, H, HD, FF = 4, 4096, 2048, 16, 128, 8192
HPC = 8
DPC = HPC * HD
TOWN = 2048
SCALE = 1.0 / math.sqrt(HD)
EPS_A = 1e-5
EPS_B = 1e-6
NEG = -1.0e9

ENGS = ("sync", "gpsimd", "vector", "scalar", "tensor")


SELF_SYNC = {"vector": "dv", "scalar": "ac", "gpsimd": "gp"}


class Prog:

    def __init__(self, nc):
        self.nc = nc
        self.ops = {e: [] for e in ENGS}
        self.cnt = {}
        self.sems = {}

    def mksems(self, ctx, names):
        for name in names:
            self.sems[name] = ctx.enter_context(self.nc.semaphore(name))
            self.cnt[name] = 0

    def op(self, eng, fn, waits=(), inc=None, incn=1, _isdma=False):
        waits = list(waits)
        if not _isdma and eng in SELF_SYNC:
            own = SELF_SYNC[eng]
            if inc is None:
                inc, incn = own, 1
            assert inc == own, (eng, inc)
            waits.append((own, self.cnt[own]))
        self.ops[eng].append((tuple(waits), fn, inc, incn))
        if inc is not None:
            self.cnt[inc] += incn
            return self.cnt[inc]
        return None

    def dma(self, eng, out, in_, waits=(), inc="qs", **kw):
        waits = list(waits) + [(inc, self.cnt[inc])]
        return self.op(eng, lambda e: e.dma_start(out=out, in_=in_, **kw),
                       waits=waits, inc=inc, incn=16, _isdma=True)

    def c(self, sem):
        return self.cnt[sem]

    def emit(self, block):
        sems = self.sems
        for eng_name in ENGS:
            def body(eng, ops=self.ops[eng_name]):
                hw = {}
                for waits, fn, inc, incn in ops:
                    for (s, v) in waits:
                        if v > 0 and hw.get(s, 0) < v:
                            eng.wait_ge(sems[s], v)
                            hw[s] = v
                    inst = fn(eng)
                    if inc is not None:
                        inst.then_inc(sems[inc], incn)
            getattr(block, eng_name)(body)


def _r3(ap2d, b):
    return ap2d.rearrange("p (a b) -> p a b", b=b)



def build_launchA():
    nc = bass.Bass("TRN2", target_bir_lowering=False)
    ap = lambda t: t.ap()
    xb = ap(nc.dram_tensor("xb", [S, D], F32, kind="ExternalInput"))
    wqT = ap(nc.dram_tensor("wqT", [D, DPC], F32, kind="ExternalInput"))
    wkT = ap(nc.dram_tensor("wkT", [D, DPC], F32, kind="ExternalInput"))
    wvT = ap(nc.dram_tensor("wvT", [D, DPC], F32, kind="ExternalInput"))
    cosf = ap(nc.dram_tensor("cosf", [HD, S], F32, kind="ExternalInput"))
    sinf = ap(nc.dram_tensor("sinf", [HD, S], F32, kind="ExternalInput"))
    rtc = ap(nc.dram_tensor("rtc", [HD, HD], BF16, kind="ExternalInput"))
    idb = ap(nc.dram_tensor("idb", [128, 128], BF16, kind="ExternalInput"))
    masks = ap(nc.dram_tensor("masks", [128, 4 * 512], F32, kind="ExternalInput"))
    gfc = ap(nc.dram_tensor("gfc", [1024, 2048], F32, kind="ExternalInput"))
    gpj = ap(nc.dram_tensor("gpj", [1024, 2048], F32, kind="ExternalInput"))

    out_norm = ap(nc.dram_tensor("out_norm", [DPC, S], BF16, kind="ExternalOutput"))
    gpart = ap(nc.dram_tensor("gpart", [1, 2], F32, kind="ExternalOutput"))

    qs_scr = ap(nc.dram_tensor("qs_scr", [HPC, HD, S], BF16))
    ks_scr = ap(nc.dram_tensor("ks_scr", [HPC, HD, S], BF16))
    vs_scr = ap(nc.dram_tensor("vs_scr", [HPC, HD, S], BF16))

    with ExitStack() as ctx:
        sb = lambda name, shape, dt: ctx.enter_context(nc.sbuf_tensor(name, shape, dt))
        psa = lambda name, shape, dt: ctx.enter_context(nc.psum_tensor(name, shape, dt))

        cos_s = sb("cos_s", [HD, S], F32)
        sin_s = sb("sin_s", [HD, S], F32)
        rt_s = sb("rt_s", [HD, HD], BF16)
        idb_s = sb("idb_s", [128, 128], BF16)
        mask_s = sb("mask_s", [128, 4 * 512], F32)
        ones1f = sb("ones1f", [1, 128], F32)
        onescol_f = sb("onescol_f", [128, 1], F32)
        onescol_b = sb("onescol_b", [128, 1], BF16)
        epsa_t = sb("epsa_t", [128, 1], F32)
        gdump = sb("gdump", [128, 2048], F32)
        gstat = sb("gstat", [128, 16], F32)
        gred = sb("gred", [128, 2], F32)
        gout = sb("gout", [1, 2], F32)

        arena = sb("arena", [128, 49152], BF16)
        ar = arena.ap()
        xn_fm = _r3(ar[:, 0:32768], 2048)
        wbf = _r3(ar[:, 32768:49152], 1024)
        kh = [ar[:, 0:4096], ar[:, 4096:8192]]
        vh = [ar[:, 8192:12288], ar[:, 12288:16384]]
        qh = [ar[:, 16384:20480], ar[:, 20480:24576]]
        vtok = ar[:, 24576:28672]
        pbuf = ar[:, 28672:45056]
        sqb = ar[:, 45056:49152]

        xt = [sb(f"xt{i}", [128, D], F32) for i in range(2)]
        stats = sb("stats", [128, 24], F32)
        mv = sb("mv", [128, 2], F32)
        rstd = sb("rstd", [128, 1], F32)
        xn_tok = sb("xn_tok", [128, D], BF16)
        raw = [sb(f"raw{i}", [128, 512], BF16) for i in range(2)]
        t1b = [sb(f"t1b{i}", [128, 512], F32) for i in range(2)]
        t2b = [sb(f"t2b{i}", [128, 512], F32) for i in range(2)]
        stg = [sb(f"stg{i}", [128, 512], BF16) for i in range(2)]

        mx8 = sb("mx8", [1, 16], F32)
        sc1 = sb("sc1", [1, 4], F32)
        negb = sb("negb", [128, 1], F32)
        rs_sb = sb("rs_sb", [1, 512], F32)
        r_sb = [sb(f"r_sb{i}", [128, 512], F32) for i in range(2)]
        ost = [sb(f"ost{i}", [128, 512], BF16) for i in range(2)]

        pacc = [psa(f"pacc{i}", [128, 512], F32) for i in range(4)]
        ptr = psa("ptr", [128, 512], BF16)
        psc = [psa(f"psc{i}", [128, 512], F32) for i in range(2)]
        prow = psa("prow", [1, 512], F32)

        p = Prog(nc)
        p.mksems(ctx, ("qs", "qg", "dv", "ac", "pe", "gp"))

        with nc.Block() as block:
            p.dma("sync", cos_s[:], cosf)
            p.dma("sync", sin_s[:], sinf)
            p.dma("sync", rt_s[:], rtc)
            p.dma("sync", idb_s[:], idb)
            p.dma("sync", mask_s[:], masks)
            consts = p.c("qs")
            p.op("vector", lambda v: v.memset(ones1f[:], 1.0), inc="dv")
            p.op("vector", lambda v: v.memset(onescol_f[:], 1.0), inc="dv")
            p.op("vector", lambda v: v.memset(onescol_b[:], 1.0), inc="dv")
            p.op("vector", lambda v: v.memset(epsa_t[:], EPS_A), inc="dv")
            memsets = p.c("dv")

            for gi, gsrc in enumerate((gfc, gpj)):
                for i in range(8):
                    w = p.dma("sync", gdump[:], gsrc[i * 128:(i + 1) * 128, :],
                              waits=[("ac", p.c("ac"))])
                    col = gi * 8 + i
                    p.op("scalar",
                         lambda sc, col=col: sc.activation(
                             gdump[:], gdump[:], AF.Abs,
                             accum_out=gstat[:, col:col + 1]),
                         waits=[("qs", w)], inc="ac")
            gabs = p.c("ac")
            p.op("vector", lambda v: v.reduce_sum(gred[:, 0:1], gstat[:, 0:8], axis=AX.X),
                 waits=[("ac", gabs)], inc="dv")
            p.op("vector", lambda v: v.reduce_sum(gred[:, 1:2], gstat[:, 8:16], axis=AX.X),
                 inc="dv")
            gredn = p.c("dv")
            p.op("tensor", lambda t: t.matmul(prow[:, 0:2], onescol_f[:], gred[:, 0:2],
                                              start=True, stop=True),
                 waits=[("dv", gredn)], inc="pe")
            gpe = p.c("pe")
            p.op("vector", lambda v: v.tensor_copy(gout[:], prow[:, 0:2]),
                 waits=[("pe", gpe)], inc="dv")
            p.dma("gpsimd", gpart, gout[:], waits=[("dv", p.c("dv"))], inc="qg")

            for st in range(2):
                t0 = st * 2048
                for ti in range(16):
                    buf = xt[ti % 2]
                    w = p.dma("sync", buf[:], xb[t0 + ti * 128: t0 + (ti + 1) * 128, :],
                              waits=[("dv", p.c("dv"))])
                    for ci in range(4):
                        p.op("vector",
                             lambda v, buf=buf, ci=ci: v.bn_stats(
                                 stats[:, ci * 6:(ci + 1) * 6],
                                 buf[:, ci * 512:(ci + 1) * 512]),
                             waits=[("qs", w), ("dv", memsets)], inc="dv")
                    p.op("vector", lambda v: v.bn_aggr(mv[:], stats[:]), inc="dv")
                    agg = p.c("dv")
                    p.op("scalar", lambda sc: sc.activation(rstd[:], mv[:, 1:2], AF.Ln,
                                                            bias=epsa_t[:]),
                         waits=[("dv", agg)], inc="ac")
                    p.op("scalar", lambda sc: sc.activation(rstd[:], rstd[:], AF.Exp,
                                                            scale=-0.5), inc="ac")
                    rst = p.c("ac")
                    p.op("vector",
                         lambda v, buf=buf: v.tensor_scalar(
                             xn_tok[:], buf[:], mv[:, 0:1], rstd[:],
                             op0=ALU.subtract, op1=ALU.mult),
                         waits=[("ac", rst)], inc="dv")
                    xnt = p.c("dv")
                    for dq in range(4):
                        def trf(t, dq=dq):
                            for j in range(4):
                                dc = dq * 4 + j
                                inst = t.transpose(ptr[:, j * 128:(j + 1) * 128],
                                                   xn_tok[:, dc * 128:(dc + 1) * 128],
                                                   idb_s[:])
                            return inst
                        p.op("tensor", trf,
                             waits=[("dv", p.c("dv")), ("qs", consts)], inc="pe")
                        trp = p.c("pe")
                        p.op("vector",
                             lambda v, dq=dq, ti=ti: v.tensor_copy(
                                 xn_fm[:, dq * 4:(dq + 1) * 4, ti * 128:(ti + 1) * 128],
                                 _r3(ptr[:, 0:512], 128)),
                             waits=[("pe", trp)], inc="dv")
                for (wsrc, scr, dorope) in ((wkT, ks_scr, True), (wvT, vs_scr, False),
                                            (wqT, qs_scr, True)):
                    wl = p.dma("gpsimd", wbf,
                               wsrc.rearrange("(a p) e -> p a e", p=128),
                               waits=[("pe", p.c("pe"))], inc="qg")
                    for h in range(HPC):
                        for tq in range(4):
                            bank = pacc[tq % 4]

                            def mmf(t, h=h, tq=tq, bank=bank):
                                for dc in range(16):
                                    inst = t.matmul(bank[:],
                                                    wbf[:, dc, h * 128:(h + 1) * 128],
                                                    xn_fm[:, dc, tq * 512:(tq + 1) * 512],
                                                    start=(dc == 0), stop=(dc == 15))
                                return inst
                            p.op("tensor", mmf,
                                 waits=[("qg", wl), ("dv", p.c("dv"))], inc="pe")
                            mmn = p.c("pe")
                            bi = (h * 4 + tq) % 2
                            if dorope:
                                p.op("vector",
                                     lambda v, bank=bank, bi=bi: v.tensor_copy(
                                         raw[bi][:], bank[:]),
                                     waits=[("pe", mmn), ("qg", p.c("qg"))], inc="dv")
                                rawn = p.c("dv")
                                p.op("tensor",
                                     lambda t, bi=bi: t.matmul(psc[0][:], rt_s[:],
                                                               raw[bi][:],
                                                               start=True, stop=True),
                                     waits=[("dv", rawn)], inc="pe")
                                rotn = p.c("pe")
                                p.op("gpsimd",
                                     lambda g, bi=bi, tq=tq, t0=t0: g.tensor_tensor(
                                         t1b[bi][:], raw[bi][:],
                                         cos_s[:, t0 + tq * 512: t0 + (tq + 1) * 512],
                                         op=ALU.mult),
                                     waits=[("dv", rawn), ("qs", consts)], inc="gp")
                                gpn = p.c("gp")
                                p.op("vector",
                                     lambda v, bi=bi, tq=tq, t0=t0: v.tensor_tensor(
                                         t2b[bi][:], psc[0][:],
                                         sin_s[:, t0 + tq * 512: t0 + (tq + 1) * 512],
                                         op=ALU.mult),
                                     waits=[("pe", rotn)], inc="dv")
                                p.op("vector",
                                     lambda v, bi=bi: v.tensor_tensor(
                                         stg[bi][:], t2b[bi][:], t1b[bi][:], op=ALU.add),
                                     waits=[("gp", gpn)], inc="dv")
                            else:
                                p.op("vector",
                                     lambda v, bank=bank, bi=bi: v.tensor_copy(
                                         stg[bi][:], bank[:]),
                                     waits=[("pe", mmn), ("qg", p.c("qg"))], inc="dv")
                            stn = p.c("dv")
                            p.dma("gpsimd",
                                  scr[h, :, t0 + tq * 512: t0 + (tq + 1) * 512],
                                  stg[bi][:], waits=[("dv", stn)], inc="qg")

            a2_qg = p.c("qg")

            av_last = 0
            for h in range(HPC):
                hb = h % 2
                arena_free = [("pe", p.c("pe")), ("dv", p.c("dv"))] if h == 0 else []
                lk = p.dma("sync", kh[hb], ks_scr[h],
                           waits=[("qg", a2_qg)] + arena_free + [("pe", av_last)])
                p.dma("sync", vh[hb], vs_scr[h])
                p.dma("sync", qh[hb], qs_scr[h])
                ld = p.c("qs")

                p.op("scalar", lambda sc, hb=hb: sc.activation(sqb, kh[hb], AF.Square),
                     waits=[("qs", ld), ("pe", p.c("pe"))], inc="ac")
                sq1 = p.c("ac")
                for i in range(8):
                    p.op("tensor",
                         lambda t, i=i: t.matmul(prow[:], onescol_b[:],
                                                 sqb[:, i * 512:(i + 1) * 512],
                                                 start=True, stop=True),
                         waits=[("ac", sq1), ("dv", p.c("dv"))], inc="pe")
                    mm = p.c("pe")
                    p.op("vector",
                         lambda v, i=i: v.reduce_max(mx8[:, i:i + 1], prow[:], axis=AX.X),
                         waits=[("pe", mm)], inc="dv")
                p.op("scalar", lambda sc, hb=hb: sc.activation(sqb, qh[hb], AF.Square),
                     waits=[("pe", p.c("pe"))], inc="ac")
                sq2 = p.c("ac")
                for i in range(8):
                    p.op("tensor",
                         lambda t, i=i: t.matmul(prow[:], onescol_b[:],
                                                 sqb[:, i * 512:(i + 1) * 512],
                                                 start=True, stop=True),
                         waits=[("ac", sq2), ("dv", p.c("dv"))], inc="pe")
                    mm = p.c("pe")
                    p.op("vector",
                         lambda v, i=i: v.reduce_max(mx8[:, 8 + i:9 + i], prow[:],
                                                     axis=AX.X),
                         waits=[("pe", mm)], inc="dv")
                p.op("vector", lambda v: v.reduce_max(sc1[:, 0:1], mx8[:, 0:8], axis=AX.X),
                     inc="dv")
                p.op("vector", lambda v: v.reduce_max(sc1[:, 1:2], mx8[:, 8:16], axis=AX.X),
                     inc="dv")
                p.op("vector", lambda v: v.tensor_tensor(sc1[:, 2:3], sc1[:, 0:1],
                                                         sc1[:, 1:2], op=ALU.mult),
                     inc="dv")
                prodd = p.c("dv")
                p.op("scalar", lambda sc: sc.activation(sc1[:, 3:4], sc1[:, 2:3], AF.Ln),
                     waits=[("dv", prodd)], inc="ac")
                p.op("scalar", lambda sc: sc.activation(sc1[:, 3:4], sc1[:, 3:4], AF.Exp,
                                                        scale=0.5), inc="ac")
                sqd = p.c("ac")
                p.op("vector", lambda v: v.tensor_scalar_mul(sc1[:, 3:4], sc1[:, 3:4],
                                                             -SCALE),
                     waits=[("ac", sqd)], inc="dv")
                nbd = p.c("dv")
                p.op("tensor", lambda t: t.matmul(psc[0][:, 0:1], ones1f[:],
                                                  sc1[0:1, 3:4], start=True, stop=True),
                     waits=[("dv", nbd)], inc="pe")
                nbp = p.c("pe")
                p.op("vector", lambda v: v.tensor_copy(negb[:], psc[0][:, 0:1]),
                     waits=[("pe", nbp)], inc="dv")
                negbd = p.c("dv")

                for cq in range(8):
                    def vtf(t, cq=cq, hb=hb):
                        for j in range(4):
                            kc = cq * 4 + j
                            inst = t.transpose(ptr[:, j * 128:(j + 1) * 128],
                                               vh[hb][:, kc * 128:(kc + 1) * 128],
                                               idb_s[:])
                        return inst
                    p.op("tensor", vtf, waits=[("qs", ld), ("dv", p.c("dv"))], inc="pe")
                    vt = p.c("pe")
                    p.op("vector",
                         lambda v, cq=cq: v.tensor_copy(vtok[:, cq * 512:(cq + 1) * 512],
                                                        ptr[:, 0:512]),
                         waits=[("pe", vt)], inc="dv")
                vtd = p.c("dv")

                for qt in range(8):
                    nch = 4 * (qt + 1)
                    exp0 = p.c("ac")
                    qt_dv0 = p.c("dv")
                    for kc in range(nch):
                        bank = psc[kc % 2]
                        wsc = exp0 if kc < 2 else exp0 + kc - 1
                        p.op("tensor",
                             lambda t, kc=kc, qt=qt, hb=hb, bank=bank: t.matmul(
                                 bank[:], kh[hb][:, kc * 128:(kc + 1) * 128],
                                 qh[hb][:, qt * 512:(qt + 1) * 512],
                                 start=True, stop=True),
                             waits=[("ac", wsc), ("dv", qt_dv0)], inc="pe")
                        smm = p.c("pe")
                        wexp = [("pe", smm), ("pe", av_last)]
                        if kc >= nch - 4:
                            r = kc - (nch - 4)
                            p.op("vector",
                                 lambda v, bank=bank, r=r: v.tensor_tensor(
                                     bank[:], bank[:],
                                     mask_s[:, r * 512:(r + 1) * 512], op=ALU.add),
                                 waits=[("pe", smm)], inc="dv")
                            wexp = [("dv", p.c("dv")), ("pe", av_last)]
                        p.op("scalar",
                             lambda sc, kc=kc, bank=bank: sc.activation(
                                 pbuf[:, kc * 512:(kc + 1) * 512], bank[:], AF.Exp,
                                 scale=SCALE, bias=negb[:]),
                             waits=wexp, inc="ac")
                    expd = p.c("ac")

                    def avf(t, nch=nch):
                        for kc in range(nch):
                            t.matmul(pacc[0][:], vtok[:, kc * 128:(kc + 1) * 128],
                                     pbuf[:, kc * 512:(kc + 1) * 512],
                                     start=(kc == 0), stop=(kc == nch - 1))
                        for kc in range(nch):
                            inst = t.matmul(prow[:], onescol_b[:],
                                            pbuf[:, kc * 512:(kc + 1) * 512],
                                            start=(kc == 0), stop=(kc == nch - 1))
                        return inst
                    p.op("tensor", avf, waits=[("ac", expd), ("dv", qt_dv0)], inc="pe")
                    av_last = p.c("pe")
                    p.op("vector", lambda v: v.reciprocal(rs_sb[:], prow[:]),
                         waits=[("pe", av_last)], inc="dv")
                    rsd = p.c("dv")
                    p.op("tensor", lambda t: t.matmul(psc[0][:], ones1f[:], rs_sb[:],
                                                      start=True, stop=True),
                         waits=[("dv", rsd), ("ac", p.c("ac"))], inc="pe")
                    rbd = p.c("pe")
                    bi = qt % 2
                    p.op("vector", lambda v, bi=bi: v.tensor_copy(r_sb[bi][:], psc[0][:]),
                         waits=[("pe", rbd)], inc="dv")
                    rsbd = p.c("dv")
                    p.op("vector",
                         lambda v, bi=bi: v.scalar_tensor_tensor(
                             ost[bi][:], pacc[0][:], 1.0, r_sb[bi][:],
                             op0=ALU.mult, op1=ALU.mult),
                         waits=[("dv", rsbd), ("qg", p.c("qg"))], inc="dv")
                    ostd = p.c("dv")
                    p.dma("gpsimd",
                          out_norm[h * 128:(h + 1) * 128, qt * 512:(qt + 1) * 512],
                          ost[bi][:], waits=[("dv", ostd)], inc="qg")

            p.op("sync", lambda e: e.nop(),
                 waits=[("qg", p.c("qg")), ("qs", p.c("qs"))])
            p.emit(block)
    return nc



def build_launchB():
    nc = bass.Bass("TRN2", target_bir_lowering=False)
    ap = lambda t: t.ap()
    agin = ap(nc.dram_tensor("agin", [D, TOWN], BF16, kind="ExternalInput"))
    xown = ap(nc.dram_tensor("xown", [TOWN, D], F32, kind="ExternalInput"))
    woT = ap(nc.dram_tensor("woT", [D, D], F32, kind="ExternalInput"))
    wfcT = ap(nc.dram_tensor("wfcT", [D, FF], F32, kind="ExternalInput"))
    wpjT = ap(nc.dram_tensor("wpjT", [FF, D], F32, kind="ExternalInput"))
    gparts = ap(nc.dram_tensor("gparts", [1, 16], F32, kind="ExternalInput"))
    idb = ap(nc.dram_tensor("idb", [128, 128], BF16, kind="ExternalInput"))
    idf = ap(nc.dram_tensor("idf", [128, 128], F32, kind="ExternalInput"))

    xout = ap(nc.dram_tensor("xout", [TOWN, D], F32, kind="ExternalOutput"))
    h_scr = ap(nc.dram_tensor("h_scr", [FF, TOWN], BF16))

    with ExitStack() as ctx:
        sb = lambda name, shape, dt: ctx.enter_context(nc.sbuf_tensor(name, shape, dt))
        sbT = lambda name, shape, dt: ctx.enter_context(
            nc.sbuf_tensor(name, shape, dt, is_transpose_dest=True))
        psa = lambda name, shape, dt: ctx.enter_context(nc.psum_tensor(name, shape, dt))

        idb_s = sb("idb_s", [128, 128], BF16)
        idf_s = sb("idf_s", [128, 128], F32)
        ones1f = sb("ones1f", [1, 128], F32)
        onescol_b = sb("onescol_b", [128, 1], BF16)
        epsa_t = sb("epsa_t", [128, 1], F32)
        epsb_t = sb("epsb_t", [128, 1], F32)
        gp_s = sb("gp_s", [1, 16], F32)
        gsc = sb("gsc", [1, 4], F32)
        gfc_b = sb("gfc_b", [128, 1], F32)
        gpj_b = sb("gpj_b", [128, 1], F32)
        gfcinv_b = sb("gfcinv_b", [128, 1], F32)
        gpjinv_b = sb("gpjinv_b", [128, 1], F32)

        arena = sb("arenaB", [128, 32768], BF16)
        xfm2 = _r3(arena.ap(), 2048)
        tpj = _r3(arena.ap(), 512)
        arf = sb("arf", [128, 8192], F32)
        h1t = [arf.ap()[:, i * 2048:(i + 1) * 2048] for i in range(4)]
        wps = [_r3(arf.ap()[:, i * 4096:(i + 1) * 4096], 512) for i in range(2)]
        arf2 = sb("arf2", [128, 8192], F32)
        xn2 = arf2.ap()[:, 0:2048]
        clipt = arf2.ap()[:, 2048:4096]
        xrow = [arf2.ap()[:, 4096 + i * 2048: 4096 + (i + 1) * 2048] for i in range(2)]
        wfs = [_r3(arf2.ap()[:, i * 2048:(i + 1) * 2048], 128) for i in range(2)]
        zfc = _r3(arf2.ap()[:, 4096:6144], 128)
        zfc2 = _r3(arf2.ap()[:, 6144:8192], 128)
        zpj = _r3(arf2.ap()[:, 0:4096], 512)
        zpj2 = _r3(arf2.ap()[:, 4096:8192], 512)

        agsl = sb("agsl", [128, 16, 512], BF16)
        wob = [sb(f"wob{i}", [128, 16, 128], BF16) for i in range(2)]
        attn_sb = [sb(f"attn_sb{i}", [128, 512], F32) for i in range(2)]
        stats = sb("stats", [128, 96], F32)
        mv = sb("mv", [128, 2], F32)
        rstd = sb("rstd", [128, 1], F32)
        gq = sb("gq", [128, 1], F32)
        s1v = sb("s1v", [128, 1], F32)
        s2v = sb("s2v", [128, 1], F32)
        xh_tok = sb("xh_tok", [128, D], BF16)
        tfc = [sb(f"tfc{i}", [128, 16, 128], BF16) for i in range(2)]
        hgel = [sb(f"hgel{i}", [128, 512], BF16) for i in range(2)]
        htok = sbT("htok", [128, FF], BF16)
        maxh = sb("maxh", [128, 1], F32)
        minh = sb("minh", [128, 1], F32)
        ga = sb("ga", [128, 1], F32)
        gb = sb("gb", [128, 1], F32)
        svec = sb("svec", [128, 16], F32)
        msv = sb("msv", [128, 1], F32)
        msrow = sb("msrow", [1, TOWN], F32)
        crow = sb("crow", [1, 512], F32)
        htile = [sb(f"htile{i}", [128, 512], BF16) for i in range(2)]
        csb = sb("csb", [128, 512], F32)
        o2sb = [sb(f"o2sb{i}", [128, 512], F32) for i in range(2)]

        pacc = [psa(f"pacc{i}", [128, 512], F32) for i in range(4)]
        ptr = psa("ptr", [128, 512], BF16)
        ptrf = psa("ptrf", [128, 512], F32)
        pcor = psa("pcor", [128, 512], F32)
        prow = psa("prow", [1, 512], F32)

        p = Prog(nc)
        p.mksems(ctx, ("qs", "qg", "qa", "dv", "ac", "pe", "gp"))

        with nc.Block() as block:
            p.dma("sync", idb_s[:], idb)
            p.dma("sync", idf_s[:], idf)
            p.dma("sync", gp_s[:], gparts)
            consts = p.c("qs")
            p.op("vector", lambda v: v.memset(ones1f[:], 1.0), inc="dv")
            p.op("vector", lambda v: v.memset(onescol_b[:], 1.0), inc="dv")
            p.op("vector", lambda v: v.memset(epsa_t[:], EPS_A), inc="dv")
            p.op("vector", lambda v: v.memset(epsb_t[:], EPS_B), inc="dv")

            p.op("vector", lambda v: v.reduce_sum(gsc[:, 0:1], gp_s[:, 0:8], axis=AX.X),
                 waits=[("qs", consts)], inc="dv")
            p.op("vector", lambda v: v.reduce_sum(gsc[:, 1:2], gp_s[:, 8:16], axis=AX.X),
                 inc="dv")
            p.op("vector", lambda v: v.tensor_scalar_mul(gsc[:, 0:2], gsc[:, 0:2],
                                                         1.0 / (D * FF)), inc="dv")
            p.op("vector", lambda v: v.tensor_scalar_add(gsc[:, 2:4], gsc[:, 0:2], EPS_B),
                 inc="dv")
            p.op("vector", lambda v: v.reciprocal(gsc[:, 2:4], gsc[:, 2:4]), inc="dv")
            gscd = p.c("dv")
            for (dst, col) in ((gfc_b, 0), (gpj_b, 1), (gfcinv_b, 2), (gpjinv_b, 3)):
                p.op("tensor",
                     lambda t, col=col: t.matmul(pcor[:, 0:1], ones1f[:],
                                                 gsc[0:1, col:col + 1],
                                                 start=True, stop=True),
                     waits=[("dv", max(gscd, p.cnt["dv"]))], inc="pe")
                mm = p.c("pe")
                p.op("vector", lambda v, dst=dst: v.tensor_copy(dst[:], pcor[:, 0:1]),
                     waits=[("pe", mm)], inc="dv")
            gbc = p.c("dv")

            for tq in range(4):
                tb = tq * 512
                wag = p.dma("sync", agsl[:],
                            agin[:, tb:tb + 512].rearrange("(a p) t -> p a t", p=128),
                            waits=[("dv", gbc), ("pe", p.c("pe"))])
                for tsub in range(4):
                    xw = p.dma("sync", xrow[tsub % 2][:],
                               xown[tb + tsub * 128: tb + (tsub + 1) * 128, :],
                               waits=[("dv", p.c("dv"))])
                    p.op("vector",
                         lambda v, tsub=tsub: v.tensor_copy(h1t[tsub], xrow[tsub % 2][:]),
                         waits=[("qs", xw), ("qa", p.cnt["qa"])], inc="dv")
                h1i = p.c("dv")
                for ec in range(16):
                    eb = ec % 2
                    wwo = p.dma("gpsimd", wob[eb][:],
                                woT[:, ec * 128:(ec + 1) * 128]
                                .rearrange("(a p) e -> p a e", p=128),
                                waits=[("pe", p.c("pe"))], inc="qg")

                    def wmm(t, eb=eb):
                        for dc in range(16):
                            inst = t.matmul(pacc[eb][:], wob[eb][:, dc, :],
                                            agsl[:, dc, :],
                                            start=(dc == 0), stop=(dc == 15))
                        return inst
                    p.op("tensor", wmm,
                         waits=[("qg", wwo), ("qs", wag), ("ac", p.c("ac"))], inc="pe")
                    wod = p.c("pe")
                    p.op("scalar",
                         lambda sc, eb=eb: sc.activation(attn_sb[eb][:], pacc[eb][:],
                                                         AF.Copy),
                         waits=[("pe", wod)], inc="ac")
                    attnd = p.c("ac")

                    def trf(t, eb=eb):
                        for tsub in range(4):
                            inst = t.transpose(ptrf[:, tsub * 128:(tsub + 1) * 128],
                                               attn_sb[eb][:, tsub * 128:(tsub + 1) * 128],
                                               idf_s[:])
                        return inst
                    p.op("tensor", trf,
                         waits=[("ac", attnd), ("dv", p.c("dv")), ("qs", consts)],
                         inc="pe")
                    trd = p.c("pe")
                    for tsub in range(4):
                        p.op("vector",
                             lambda v, tsub=tsub, ec=ec: v.tensor_tensor(
                                 h1t[tsub][:, ec * 128:(ec + 1) * 128],
                                 h1t[tsub][:, ec * 128:(ec + 1) * 128],
                                 ptrf[:, tsub * 128:(tsub + 1) * 128], op=ALU.add),
                             waits=[("pe", trd), ("dv", h1i)], inc="dv")
                h1d = p.c("dv")
                for tsub in range(4):
                    t128 = tq * 4 + tsub
                    w = p.dma("scalar", xout[tb + tsub * 128: tb + (tsub + 1) * 128, :],
                              h1t[tsub], waits=[("dv", h1d)], inc="qa")
                    for ci in range(4):
                        p.op("vector",
                             lambda v, tsub=tsub, ci=ci: v.bn_stats(
                                 stats[:, ci * 6:(ci + 1) * 6],
                                 h1t[tsub][:, ci * 512:(ci + 1) * 512]),
                             waits=[("dv", h1d)], inc="dv")
                    p.op("vector", lambda v: v.bn_aggr(mv[:], stats[:, 0:24]), inc="dv")
                    agg = p.c("dv")
                    p.op("scalar", lambda sc: sc.activation(rstd[:], mv[:, 1:2], AF.Ln,
                                                            bias=epsa_t[:]),
                         waits=[("dv", agg)], inc="ac")
                    p.op("scalar", lambda sc: sc.activation(rstd[:], rstd[:], AF.Exp,
                                                            scale=-0.5), inc="ac")
                    rst = p.c("ac")
                    p.op("vector",
                         lambda v, tsub=tsub: v.tensor_scalar(
                             xn2, h1t[tsub], mv[:, 0:1], rstd[:],
                             op0=ALU.subtract, op1=ALU.mult),
                         waits=[("ac", rst)], inc="dv")
                    p.op("vector", lambda v: v.reduce_max(gq[:], xn2, axis=AX.X,
                                                          apply_absolute_value=True),
                         inc="dv")
                    p.op("vector", lambda v: v.reciprocal(s1v[:], gq[:]), inc="dv")
                    p.op("vector", lambda v: v.tensor_scalar_mul(s1v[:], s1v[:], 128.0),
                         inc="dv")
                    p.op("vector", lambda v: v.tensor_scalar_mul(s2v[:], gq[:],
                                                                 1.0 / 128.0), inc="dv")
                    p.op("vector", lambda v: v.tensor_scalar(clipt, xn2, s1v[:], 128.0,
                                                             op0=ALU.mult, op1=ALU.min),
                         inc="dv")
                    p.op("vector", lambda v: v.tensor_scalar(xh_tok[:], clipt, -128.0,
                                                             s2v[:], op0=ALU.max,
                                                             op1=ALU.mult), inc="dv")
                    xhd = p.c("dv")
                    for dq in range(4):
                        def trx(t, dq=dq):
                            for j in range(4):
                                dc = dq * 4 + j
                                inst = t.transpose(ptr[:, j * 128:(j + 1) * 128],
                                                   xh_tok[:, dc * 128:(dc + 1) * 128],
                                                   idb_s[:])
                            return inst
                        p.op("tensor", trx, waits=[("dv", p.cnt["dv"])], inc="pe")
                        trp = p.c("pe")
                        p.op("vector",
                             lambda v, dq=dq, t128=t128: v.tensor_copy(
                                 xfm2[:, dq * 4:(dq + 1) * 4,
                                      t128 * 128:(t128 + 1) * 128],
                                 _r3(ptr[:, 0:512], 128)),
                             waits=[("pe", trp)], inc="dv")
            xfm2d = p.c("dv")
            h1_stores = p.c("qa")

            for oc in range(64):
                ob = oc % 2
                wld = p.dma("sync", wfs[ob][:],
                            wfcT[:, oc * 128:(oc + 1) * 128]
                            .rearrange("(a p) n -> p a n", p=128),
                            waits=[("dv", xfm2d if oc < 2 else p.cnt["dv"]),
                                   ("pe", p.c("pe"))])

                wf = wfs[ob].rearrange("p a b -> p (a b)")
                p.op("vector",
                     lambda v, wf=wf: v.tensor_scalar(
                         zfc.rearrange("p a b -> p (a b)"), wf, gfcinv_b[:], 0.5,
                         op0=ALU.mult, op1=ALU.is_ge),
                     waits=[("qs", wld), ("pe", p.c("pe"))], inc="dv")
                p.op("vector",
                     lambda v, wf=wf: v.tensor_scalar(
                         zfc2.rearrange("p a b -> p (a b)"), wf, gfcinv_b[:], -0.5,
                         op0=ALU.mult, op1=ALU.is_le), inc="dv")
                p.op("vector",
                     lambda v, ob=ob: v.tensor_tensor(
                         tfc[ob][:].rearrange("p a b -> p (a b)"),
                         zfc.rearrange("p a b -> p (a b)"),
                         zfc2.rearrange("p a b -> p (a b)"), op=ALU.subtract),
                     inc="dv")
                ternd = p.c("dv")

                def fcm(t, ob=ob):
                    for dc in range(16):
                        for tq in range(4):
                            inst = t.matmul(pacc[tq][:], tfc[ob][:, dc, :],
                                            xfm2[:, dc, tq * 512:(tq + 1) * 512],
                                            start=(dc == 0), stop=(dc == 15))
                    return inst
                p.op("tensor", fcm, waits=[("dv", ternd), ("ac", p.c("ac"))], inc="pe")
                fcd = p.c("pe")
                for tq in range(4):
                    hb = tq % 2
                    p.op("scalar",
                         lambda sc, tq=tq, hb=hb: sc.activation(
                             hgel[hb][:], pacc[tq][:], AF.Gelu, scale=gfc_b[:]),
                         waits=[("pe", fcd), ("qa", p.c("qa"))], inc="ac")
                    geld = p.c("ac")
                    p.dma("scalar", h_scr[oc * 128:(oc + 1) * 128,
                                          tq * 512:(tq + 1) * 512],
                          hgel[hb][:], waits=[("ac", geld)], inc="qa")
            h_stores = p.c("qa")

            for ti in range(16):
                w = p.dma("sync", htok[:], h_scr[:, ti * 128:(ti + 1) * 128],
                          waits=[("qa", h_stores), ("dv", p.c("dv"))], transpose=True)
                for ci in range(16):
                    p.op("vector",
                         lambda v, ci=ci: v.bn_stats(stats[:, ci * 6:(ci + 1) * 6],
                                                     htok[:, ci * 512:(ci + 1) * 512]),
                         waits=[("qs", w)], inc="dv")
                p.op("vector", lambda v: v.bn_aggr(mv[:], stats[:]), inc="dv")
                p.op("vector", lambda v: v.reduce_max(maxh[:], htok[:], axis=AX.X),
                     inc="dv")
                p.op("vector", lambda v: v.tensor_reduce(minh[:], htok[:], axis=AX.X,
                                                         op=ALU.min), inc="dv")
                agg = p.c("dv")
                p.op("scalar", lambda sc: sc.activation(rstd[:], mv[:, 1:2], AF.Ln,
                                                        bias=epsb_t[:]),
                     waits=[("dv", agg)], inc="ac")
                p.op("scalar", lambda sc: sc.activation(rstd[:], rstd[:], AF.Exp,
                                                        scale=-0.5), inc="ac")
                rst = p.c("ac")
                p.op("vector", lambda v: v.tensor_scalar(ga[:], maxh[:], mv[:, 0:1], None,
                                                         op0=ALU.subtract),
                     waits=[("ac", rst)], inc="dv")
                p.op("vector", lambda v: v.tensor_scalar(gb[:], minh[:], mv[:, 0:1], -1.0,
                                                         op0=ALU.subtract, op1=ALU.mult),
                     inc="dv")
                p.op("vector", lambda v: v.tensor_tensor(gq[:], ga[:], gb[:], op=ALU.max),
                     inc="dv")
                p.op("vector", lambda v: v.tensor_scalar(gq[:], gq[:], rstd[:], None,
                                                         op0=ALU.mult), inc="dv")
                p.op("vector", lambda v: v.reciprocal(s1v[:], gq[:]), inc="dv")
                p.op("vector", lambda v: v.tensor_scalar_mul(s1v[:], s1v[:], 128.0),
                     inc="dv")
                p.op("vector", lambda v: v.tensor_scalar_mul(s2v[:], gq[:], 1.0 / 128.0),
                     inc="dv")
                p.op("vector", lambda v: v.tensor_scalar(s1v[:], s1v[:], s2v[:], rstd[:],
                                                         op0=ALU.mult, op1=ALU.mult),
                     inc="dv")
                p.op("vector",
                     lambda v, ti=ti: v.tensor_scalar(svec[:, ti:ti + 1], s1v[:],
                                                      gpj_b[:], None, op0=ALU.mult),
                     waits=[("dv", gbc)], inc="dv")
                p.op("vector",
                     lambda v, ti=ti: v.tensor_scalar(msv[:], svec[:, ti:ti + 1],
                                                      mv[:, 0:1], -1.0,
                                                      op0=ALU.mult, op1=ALU.mult),
                     inc="dv")
                msd = p.c("dv")
                p.op("tensor", lambda t: t.matmul(prow[:, 0:128], msv[:], idf_s[:],
                                                  start=True, stop=True),
                     waits=[("dv", msd)], inc="pe")
                msp = p.c("pe")
                p.op("vector",
                     lambda v, ti=ti: v.tensor_copy(msrow[:, ti * 128:(ti + 1) * 128],
                                                    prow[:, 0:128]),
                     waits=[("pe", msp)], inc="dv")
            p7a_d = p.c("dv")

            for eb in range(4):
                ebc = eb * 512
                for half in range(8):
                    hb = half % 2
                    wld = p.dma("sync", wps[hb],
                                wpjT[half * 1024:(half + 1) * 1024, ebc:ebc + 512]
                                .rearrange("(a p) n -> p a n", p=128),
                                waits=[("dv", p7a_d if (eb == 0 and half < 2)
                                        else p.cnt["dv"]), ("pe", p.c("pe"))])

                    wfp = wps[hb].rearrange("p a b -> p (a b)")
                    p.op("vector",
                         lambda v, wfp=wfp: v.tensor_scalar(
                             zpj.rearrange("p a b -> p (a b)"), wfp, gpjinv_b[:], 0.5,
                             op0=ALU.mult, op1=ALU.is_ge),
                         waits=[("qs", wld), ("pe", p.c("pe"))], inc="dv")
                    p.op("vector",
                         lambda v, wfp=wfp: v.tensor_scalar(
                             zpj2.rearrange("p a b -> p (a b)"), wfp, gpjinv_b[:],
                             -0.5, op0=ALU.mult, op1=ALU.is_le), inc="dv")
                    p.op("vector",
                         lambda v, half=half: v.tensor_tensor(
                             tpj[:, half * 8:(half + 1) * 8, :].rearrange(
                                 "p a b -> p (a b)"),
                             zpj.rearrange("p a b -> p (a b)"),
                             zpj2.rearrange("p a b -> p (a b)"), op=ALU.subtract),
                         inc="dv")
                ternd = p.c("dv")
                def csf(t):
                    for oc in range(64):
                        inst = t.matmul(prow[:], onescol_b[:], tpj[:, oc, :],
                                        start=(oc == 0), stop=(oc == 63))
                    return inst
                p.op("tensor", csf, waits=[("dv", ternd)], inc="pe")
                csd = p.c("pe")
                p.op("vector", lambda v: v.tensor_copy(crow[:], prow[:]),
                     waits=[("pe", csd)], inc="dv")
                crd = p.c("dv")

                for tq in range(4):
                    hw = [None] * 64
                    hw[0] = p.dma("gpsimd", htile[0][:],
                                  h_scr[0:128, tq * 512:(tq + 1) * 512],
                                  waits=[("qa", h_stores)], inc="qg")
                    hw[1] = p.dma("gpsimd", htile[1][:],
                                  h_scr[128:256, tq * 512:(tq + 1) * 512], inc="qg")
                    pe_oc = [None] * 64
                    for oc in range(64):
                        def pjm(t, oc=oc):
                            for ts4 in range(4):
                                inst = t.matmul(pacc[ts4][:],
                                                htile[oc % 2][:, ts4 * 128:(ts4 + 1) * 128],
                                                tpj[:, oc, :],
                                                start=(oc == 0), stop=(oc == 63))
                            return inst
                        p.op("tensor", pjm,
                             waits=[("qg", hw[oc]), ("dv", crd),
                                    ("dv", p.cnt["dv"] if oc == 0 else 0)], inc="pe")
                        pe_oc[oc] = p.c("pe")
                        if oc + 2 < 64:
                            hw[oc + 2] = p.dma(
                                "gpsimd", htile[oc % 2][:],
                                h_scr[(oc + 2) * 128:(oc + 3) * 128,
                                      tq * 512:(tq + 1) * 512],
                                waits=[("pe", pe_oc[oc])], inc="qg")
                    pjd = p.c("pe")
                    for ts4 in range(4):
                        t128 = tq * 4 + ts4
                        p.op("tensor",
                             lambda t, t128=t128: t.matmul(
                                 pcor[:], msrow[0:1, t128 * 128:(t128 + 1) * 128],
                                 crow[:], start=True, stop=True),
                             waits=[("dv", p.cnt["dv"]), ("pe", pjd)],
                             inc="pe")
                        cord = p.c("pe")
                        p.op("vector", lambda v: v.tensor_copy(csb[:], pcor[:]),
                             waits=[("pe", cord)], inc="dv")
                        csbd = p.c("dv")
                        ob = ts4 % 2
                        p.op("vector",
                             lambda v, ts4=ts4, t128=t128, ob=ob: v.scalar_tensor_tensor(
                                 o2sb[ob][:], pacc[ts4][:], svec[:, t128:t128 + 1],
                                 csb[:], op0=ALU.mult, op1=ALU.add),
                             waits=[("dv", csbd), ("qg", p.c("qg")),
                                    ("pe", pjd)], inc="dv")
                        o2d = p.c("dv")
                        p.dma("gpsimd",
                              xout[t128 * 128:(t128 + 1) * 128, ebc:ebc + 512],
                              o2sb[ob][:],
                              waits=[("dv", o2d), ("qa", h1_stores)],
                              inc="qg", accum_op=ALU.add)

            p.op("sync", lambda e: e.nop(),
                 waits=[("qg", p.c("qg")), ("qa", p.c("qa")), ("qs", p.c("qs"))])
            p.emit(block)
    return nc



def _host_constants():
    inv = (np.float32(1.0) /
           np.float32(10000.0) ** (np.arange(0, HD, 2, dtype=np.float32) /
                                   np.float32(HD)))
    t = np.arange(S, dtype=np.float32)
    freqs = np.outer(t, inv).astype(np.float32)
    emb = np.concatenate([freqs, freqs], axis=1)
    cosf = np.ascontiguousarray(np.cos(emb).T.astype(np.float32))
    sinf = np.ascontiguousarray(np.sin(emb).T.astype(np.float32))
    rt = np.zeros((HD, HD), dtype=ml_dtypes.bfloat16)
    for i in range(64):
        rt[i + 64, i] = -1
        rt[i, i + 64] = 1
    idb = np.eye(128, dtype=ml_dtypes.bfloat16)
    idf = np.eye(128, dtype=np.float32)
    masks = np.zeros((128, 4 * 512), dtype=np.float32)
    pp = np.arange(128)[:, None]
    nn = np.arange(512)[None, :]
    for r in range(4):
        masks[:, r * 512:(r + 1) * 512] = np.where(r * 128 + pp <= nn, 0.0, NEG)
    return cosf, sinf, rt, idb, idf, masks


def kernel(L0_attn_x, L0_attn_Wq, L0_attn_Wk, L0_attn_Wv, L0_attn_Wo,
           L0_ffn_w_fc, L0_ffn_w_proj):
    x = np.ascontiguousarray(np.asarray(L0_attn_x, dtype=np.float32))
    Wq = np.asarray(L0_attn_Wq, dtype=np.float32)
    Wk = np.asarray(L0_attn_Wk, dtype=np.float32)
    Wv = np.asarray(L0_attn_Wv, dtype=np.float32)
    Wo = np.asarray(L0_attn_Wo, dtype=np.float32)
    wfc = np.asarray(L0_ffn_w_fc, dtype=np.float32)
    wpj = np.asarray(L0_ffn_w_proj, dtype=np.float32)
    cosf, sinf, rt, idb, idf, masks = _host_constants()

    ncA = build_launchA()
    in_mapsA = []
    for c in range(8):
        b, g = c // 2, c % 2
        rows = slice(g * DPC, (g + 1) * DPC)
        in_mapsA.append({
            "xb": x[b],
            "wqT": np.ascontiguousarray(Wq[rows, :].T),
            "wkT": np.ascontiguousarray(Wk[rows, :].T),
            "wvT": np.ascontiguousarray(Wv[rows, :].T),
            "cosf": cosf, "sinf": sinf, "rtc": rt, "idb": idb, "masks": masks,
            "gfc": np.ascontiguousarray(wfc.reshape(8, 1024, 2048)[c]),
            "gpj": np.ascontiguousarray(wpj.reshape(8, 1024, 2048)[c]),
        })
    resA = run_bass_kernel_spmd(ncA, in_mapsA, core_ids=list(range(8)))
    outsA = [r["out_norm"] for r in resA.results]
    gparts = np.stack([r["gpart"][0] for r in resA.results])
    gp16 = np.ascontiguousarray(
        np.concatenate([gparts[:, 0], gparts[:, 1]]).reshape(1, 16).astype(np.float32))

    ncB = build_launchB()
    woTh = np.ascontiguousarray(Wo.T)
    wfcTh = np.ascontiguousarray(wfc.T)
    wpjTh = np.ascontiguousarray(wpj.T)
    in_mapsB = []
    for c in range(8):
        b, g = c // 2, c % 2
        cols = slice(g * TOWN, (g + 1) * TOWN)
        ag = np.ascontiguousarray(np.concatenate(
            [outsA[2 * b][:, cols], outsA[2 * b + 1][:, cols]], axis=0))
        in_mapsB.append({
            "agin": ag,
            "xown": np.ascontiguousarray(x[b, cols]),
            "woT": woTh, "wfcT": wfcTh, "wpjT": wpjTh,
            "gparts": gp16, "idb": idb, "idf": idf,
        })
    resB = run_bass_kernel_spmd(ncB, in_mapsB, core_ids=list(range(8)))
    out = np.empty((B, S, D), dtype=np.float32)
    for c in range(8):
        b, g = c // 2, c % 2
        out[b, g * TOWN:(g + 1) * TOWN] = resB.results[c]["xout"]
    return out
